# version 18
# baseline (speedup 1.0000x reference)
import numpy as np
from contextlib import ExitStack

import concourse.bass as bass
import concourse.mybir as mybir
from concourse.bass_utils import run_bass_kernel_spmd

F32 = mybir.dt.float32
BF16 = mybir.dt.bfloat16
I32 = mybir.dt.int32
AF = mybir.ActivationFunctionType
ALU = mybir.AluOpType
AX = mybir.AxisListType

D = 1024
JC = 8
F = 4096
NC_F = 32
E = 8
R = 2048
HALO_L, HALO_R = 256, 128
RT = HALO_L + R + HALO_R
NT = RT // 128
NM = R // 128
T_PASS = 512
NPASS = R // T_PASS
G = 2
NGRP = NC_F // G
NH = 8
ATTN_STD = [1.0, 2.0, 4.0, 8.0, 1.0, 2.0, 4.0, 8.0]
ATTN_OFF = [-1.0, -2.0, -4.0, -8.0, -1.0, -2.0, -4.0, -8.0]
NDELTA = 4

DEBUG = False


class Sem:
    def __init__(self, es, nc, name):
        self.h = es.enter_context(nc.semaphore(name))
        self.n = 0

    def inc(self, inst, k=1):
        inst.then_inc(self.h, k)
        self.n += k
        return self.n


def build_program():
    nc = bass.Bass("TRN2", target_bir_lowering=False)

    x_h = nc.dram_tensor("x", [RT, D], F32, kind="ExternalInput")
    wi_h = nc.dram_tensor("wi", [D, D], F32, kind="ExternalInput")
    wo_h = nc.dram_tensor("wo", [D, D], F32, kind="ExternalInput")
    wg_h = nc.dram_tensor("wg", [E, D, F], F32, kind="ExternalInput")
    wu_h = nc.dram_tensor("wu", [E, D, F], F32, kind="ExternalInput")
    wd_h = nc.dram_tensor("wd", [E, F, D], F32, kind="ExternalInput")
    out_h = nc.dram_tensor("out", [R, D], F32, kind="ExternalOutput")
    wgs_h = nc.dram_tensor("wgs", [E, NGRP, 128, JC * G * 128], BF16, kind="Internal")
    wus_h = nc.dram_tensor("wus", [E, NGRP, 128, JC * G * 128], BF16, kind="Internal")
    if DEBUG:
        dbg_vtok = nc.dram_tensor("dbg_vtok", [128, NT, D], BF16, kind="ExternalOutput")
        dbg_attT = nc.dram_tensor("dbg_attT", [128, JC, R], BF16, kind="ExternalOutput")
        dbg_g = nc.dram_tensor("dbg_g", [128, NM, D], F32, kind="ExternalOutput")
        dbg_H = nc.dram_tensor("dbg_H", [128, NC_F, T_PASS], BF16, kind="ExternalOutput")
        dbg_xnT2 = nc.dram_tensor("dbg_xnT2", [128, JC, RT], BF16, kind="ExternalOutput")

    with ExitStack() as es:
        g_sb = es.enter_context(nc.sbuf_tensor("g_sb", [128, NM, D], F32))
        x_t = es.enter_context(nc.sbuf_tensor("x_t", [128, 2, D], F32))
        xc = es.enter_context(nc.sbuf_tensor("xc", [128, 2, D], F32))
        stats = es.enter_context(nc.sbuf_tensor("stats", [128, 2, 16], F32))
        ident = es.enter_context(nc.sbuf_tensor("ident", [128, 128], BF16))
        io_f = es.enter_context(nc.sbuf_tensor("io_f", [128, NDELTA, 128], F32, side="right"))
        eps_c = es.enter_context(nc.sbuf_tensor("eps_c", [128, 1], F32))
        es_xnT = es.enter_context(ExitStack())
        xnT = es_xnT.enter_context(nc.sbuf_tensor("xnT", [128, JC, RT], BF16, side="right"))
        es_w2 = ExitStack()
        wgb2 = es_w2.enter_context(nc.sbuf_tensor("wgb2", [128, 2, JC, G * 128], BF16, side="right"))
        wub2 = es_w2.enter_context(nc.sbuf_tensor("wub2", [128, 2, JC, G * 128], BF16, side="right"))
        wio_sb = es.enter_context(nc.sbuf_tensor("wio_sb", [128, 2, D], BF16))
        ptmp = es.enter_context(nc.sbuf_tensor("ptmp", [128, 128], F32))
        es_xnb = ExitStack()
        xn_b = es_xnb.enter_context(nc.sbuf_tensor("xn_b", [128, 2, D], BF16))
        es_io = ExitStack()
        io_i = es_io.enter_context(nc.sbuf_tensor("io_i", [128, NDELTA, 128], I32))
        es_ffn = es.enter_context(ExitStack())

        psum = es.enter_context(nc.psum_tensor("psum", [128, 4096], F32))

        def bank(i, w=512):
            return psum[:, 512 * i: 512 * i + w]

        psum_bf = psum[:, :].bitcast(BF16)

        def bank_bf(i):
            return psum_bf[:, 1024 * i: 1024 * (i + 1)]

        s_gp = Sem(es, nc, "s_gp")
        s_x = [Sem(es, nc, "s_x0"), Sem(es, nc, "s_x1")]
        s_p0v = Sem(es, nc, "s_p0v")
        s_p0s = Sem(es, nc, "s_p0s")
        s_tp = Sem(es, nc, "s_tp")
        s_xnT = Sem(es, nc, "s_xnT")
        s_pt = Sem(es, nc, "s_pt")
        s_uwld = [Sem(es, nc, f"s_uwld{i}") for i in range(4)]
        s_uwldh = [Sem(es, nc, f"s_uwldh{i}") for i in range(4)]
        s_dwld = [Sem(es, nc, f"s_dwld{i}") for i in range(3)]
        s_dcons = Sem(es, nc, "s_dcons")
        s_wst_u = [Sem(es, nc, f"s_wst_u{i}") for i in range(4)]
        s_wst_d = [Sem(es, nc, f"s_wst_d{i}") for i in range(3)]
        s_ups = Sem(es, nc, "s_ups")
        s_sl = Sem(es, nc, "s_sl")
        s_h = Sem(es, nc, "s_h")
        s_dps = Sem(es, nc, "s_dps")
        s_gacc = Sem(es, nc, "s_gacc")
        s_wio = [Sem(es, nc, "s_wio0"), Sem(es, nc, "s_wio1")]
        s_wiop = Sem(es, nc, "s_wiop")
        s_wiot = Sem(es, nc, "s_wiot")
        s_vps = Sem(es, nc, "s_vps")
        s_vt = Sem(es, nc, "s_vt")
        s_aps = Sem(es, nc, "s_aps")
        s_att = Sem(es, nc, "s_att")
        s_atp = Sem(es, nc, "s_atp")
        s_attT = Sem(es, nc, "s_attT")
        s_sps = Sem(es, nc, "s_sps")
        s_fs = Sem(es, nc, "s_fs")
        s_fv = Sem(es, nc, "s_fv")
        s_out = [Sem(es, nc, "s_out0"), Sem(es, nc, "s_out1")]

        s_gp.inc(nc.gpsimd.memset(ident[:, :], 0.0))
        nc.gpsimd.wait_ge(s_gp.h, 1)
        s_gp.inc(nc.gpsimd.affine_select(
            out=ident[:, :], in_=ident[:, :], compare_op=ALU.not_equal,
            fill=1.0, base=0, pattern=[[-1, 128]], channel_multiplier=1))
        for di in range(NDELTA):
            s_gp.inc(nc.gpsimd.iota(io_i[:, di, :], pattern=[[-1, 128]], base=128 * (di - 2),
                                    channel_multiplier=1))
        nc.gpsimd.wait_ge(s_gp.h, 6)
        s_gp.inc(nc.gpsimd.tensor_copy(io_f[:, :, :], io_i[:, :, :]))
        s_gp.inc(nc.gpsimd.memset(eps_c[:, :], 1e-5))
        es_io.close()

        P0_ORDER_SYNC = [2, 3, 4, 5, 0, 1] + list(range(6, NT))
        for pos, lt in enumerate(P0_ORDER_SYNC):
            if pos >= 2:
                nc.sync.wait_ge(s_p0v.h, 5 * (pos - 2) + 5)
            s_x[pos % 2].inc(nc.sync.dma_start(x_t[:, pos % 2, :], x_h[128 * lt:128 * (lt + 1), :]), 16)
        if DEBUG:
            nc.sync.wait_ge(s_xnT.h, NT)
            s_out[0].inc(nc.sync.dma_start(dbg_xnT2[:, :, :], xnT[:, :, :]), 16)

        P0_ORDER = [2, 3, 4, 5, 0, 1] + list(range(6, NT))
        for pos, lt in enumerate(P0_ORDER):
            p = pos % 2
            nc.vector.wait_ge(s_x[p].h, 16 * (pos // 2 + 1))
            s_p0v.inc(nc.vector.bn_stats(stats[:, p, 4:10], x_t[:, p, 0:512]))
            s_p0v.inc(nc.vector.bn_stats(stats[:, p, 10:16], x_t[:, p, 512:1024]))
            nc.vector.wait_ge(s_p0v.h, 5 * pos + 2)
            s_p0v.inc(nc.vector.bn_aggr(stats[:, p, 0:2], stats[:, p, 4:16]))
            nc.vector.wait_ge(s_p0s.h, pos + 1)
            s_p0v.inc(nc.vector.reciprocal(stats[:, p, 3:4], stats[:, p, 2:3]))
            nc.vector.wait_ge(s_p0v.h, 5 * pos + 4)
            if pos >= 2:
                nc.vector.wait_ge(s_tp.h, pos - 1)
            s_p0v.inc(nc.vector.tensor_scalar(xn_b[:, p, :], x_t[:, p, :],
                                              stats[:, p, 0:1], stats[:, p, 3:4],
                                              op0=ALU.subtract, op1=ALU.mult))
            if pos == 0:
                nc.scalar.wait_ge(s_gp.h, 8)
            nc.scalar.wait_ge(s_p0v.h, 5 * pos + 3)
            s_p0s.inc(nc.scalar.activation(stats[:, p, 2:3], stats[:, p, 1:2], AF.Sqrt,
                                           bias=eps_c[:, 0:1], scale=1.0))
            nc.tensor.wait_ge(s_p0v.h, 5 * pos + 5)
            if pos == 0:
                nc.tensor.wait_ge(s_gp.h, 2)
            if pos >= 2:
                nc.tensor.wait_ge(s_xnT.h, pos - 1)
            tb = bank_bf(4 + p)
            for jc in range(JC):
                inst = nc.tensor.transpose(tb[:, 128 * jc:128 * (jc + 1)],
                                           xn_b[:, p, 128 * jc:128 * (jc + 1)], ident[:, :])
            s_tp.inc(inst)
            nc.vector.wait_ge(s_tp.h, pos + 1)
            dst = xnT[:, :, 128 * lt:128 * (lt + 1)]
            srcv = tb.rearrange("p (a f) -> p a f", a=JC)
            s_xnT.inc(nc.vector.tensor_copy(dst, srcv))

        PT_TOTAL = NH * NDELTA * 3


        es_xnb.close()
        wdb2 = es_ffn.enter_context(nc.sbuf_tensor("wdb2", [128, 1, G, D], BF16))
        wgb = es_ffn.enter_context(nc.sbuf_tensor("wgb", [128, 2, JC, G * 128], BF16))
        wub = es_ffn.enter_context(nc.sbuf_tensor("wub", [128, 2, JC, G * 128], BF16))
        wdb = es_ffn.enter_context(nc.sbuf_tensor("wdb", [128, 2, G, D], BF16))

        def wg_buf(par):
            return wgb[:, par, :, :] if par < 2 else wgb2[:, par - 2, :, :]

        def wu_buf(par):
            return wub[:, par, :, :] if par < 2 else wub2[:, par - 2, :, :]

        def wd_buf(par):
            return wdb[:, par, :, :] if par < 2 else wdb2[:, 0, :, :]

        NEG = NPASS * E
        issue = [("U", 0, cg) for cg in range(NGRP)]
        for eg in range(NEG):
            issue += [("D", eg, cg) for cg in range(NGRP - 2)]
            if eg + 1 < NEG:
                issue += [("U", eg + 1, 0), ("U", eg + 1, 1)]
            issue += [("D", eg, NGRP - 2), ("D", eg, NGRP - 1)]
            if eg + 1 < NEG:
                issue += [("U", eg + 1, cg) for cg in range(2, NGRP)]
        assert len(issue) == 2 * NEG * NGRP

        nc.gpsimd.wait_ge(s_gp.h, 8)
        first_d2 = True
        for kind, eg, cg in issue:
            if eg >= E and kind == "U":
                continue
            e = eg % E
            gidx = eg * NGRP + cg
            par = gidx % 3
            upar = gidx % 4
            c0 = cg * G * 128
            if kind == "U":
                if gidx >= 4:
                    ep, k = divmod(gidx - 4, NGRP)
                    nc.gpsimd.wait_ge(s_ups.h, ep * 2 * NC_F + 4 * k + 4)
                    nc.gpsimd.wait_ge(s_wst_u[upar].h, 32 * (gidx // 4))
                src = wg_h[e, :, c0:c0 + G * 128].rearrange("(a p) f -> p a f", p=128)
                s_uwld[upar].inc(nc.gpsimd.dma_start(wg_buf(upar), src), 16)
                src = wu_h[e, :, c0:c0 + G * 128].rearrange("(a p) f -> p a f", p=128)
                s_uwld[upar].inc(nc.gpsimd.dma_start(wu_buf(upar), src), 16)
            else:
                if par == 2 and first_d2:
                    nc.gpsimd.wait_ge(s_xnT.h, NT)
                    first_d2 = False
                if gidx >= 3:
                    ep, k = divmod(gidx - 3, NGRP)
                    if k < NGRP - 1:
                        nc.gpsimd.wait_ge(s_dcons.h, ep * (NGRP - 1) + k + 1)
                    else:
                        nc.gpsimd.wait_ge(s_dps.h, 8 * (ep + 1))
                src = wd_h[e, c0:c0 + G * 128, :].rearrange("(a p) d -> p a d", p=128)
                s_dwld[par].inc(nc.gpsimd.dma_start(wd_buf(par), src), 16)

        def wg_flat(par):
            return wg_buf(par).rearrange("p b f -> p (b f)")

        def wu_flat(par):
            return wu_buf(par).rearrange("p b f -> p (b f)")

        def wd_flat(par):
            return wd_buf(par).rearrange("p b f -> p (b f)")

        for kind, eg, cg in issue:
            e = eg % E
            gidx = eg * NGRP + cg
            upar = gidx % 4
            if kind != "U":
                continue
            if eg < E:
                nc.sync.wait_ge(s_uwld[upar].h, 32 * (gidx // 4 + 1))
                s_wst_u[upar].inc(nc.sync.dma_start(wgs_h[e, cg], wg_flat(upar)), 16)
                s_wst_u[upar].inc(nc.sync.dma_start(wus_h[e, cg], wu_flat(upar)), 16)
            else:
                ep, k = divmod(gidx - 4, NGRP)
                nc.sync.wait_ge(s_ups.h, ep * 2 * NC_F + 4 * k + 4)
                if gidx < NGRP * E + 4:
                    nc.sync.wait_ge(s_wst_u[upar].h, 32 * (gidx // 4))
                s_uwldh[upar].inc(nc.sync.dma_start(wg_flat(upar), wgs_h[e, cg]), 16)
                s_uwldh[upar].inc(nc.sync.dma_start(wu_flat(upar), wus_h[e, cg]), 16)

        x_base = [s_x[0].n, s_x[1].n]
        for m in range(NM):
            if m >= 2:
                nc.sync.wait_ge(s_fv.h, 2 * (m - 2) + 2)
                nc.sync.wait_ge(s_out[m % 2].h, 16 * (m // 2))
            else:
                last_pos = NT - 1 - ((NT - 1 + m) % 2)
                nc.sync.wait_ge(s_p0v.h, 5 * last_pos + 5)
            r0 = HALO_L + 128 * m
            s_x[m % 2].inc(nc.sync.dma_start(x_t[:, m % 2, :], x_h[r0:r0 + 128, :]), 16)
            nc.sync.wait_ge(s_fv.h, 2 * m + 2)
            s_out[m % 2].inc(nc.sync.dma_start(out_h[128 * m:128 * (m + 1), :], x_t[:, m % 2, :]), 16)

        for ic in range(16):
            if ic >= 2:
                nc.gpsimd.wait_ge(s_wiop.h, ic - 1)
            src_h = wi_h if ic < 8 else wo_h
            r0 = (ic % 8) * 128
            s_wio[ic % 2].inc(nc.gpsimd.dma_start(wio_sb[:, ic % 2, :], src_h[r0:r0 + 128, :]), 16)


        H_sb = es_ffn.enter_context(nc.sbuf_tensor("H_sb", [128, NC_F, T_PASS], BF16))
        tmp_sl = xc
        if DEBUG:
            nc.gpsimd.wait_ge(s_h.h, NPASS * E * NC_F)
            s_out[0].inc(nc.gpsimd.dma_start(dbg_H[:, :, :], H_sb[:, :, :]), 16)

        for pss in range(NPASS):
            tok0 = HALO_L + T_PASS * pss
            for e in range(E):
                eg = pss * E + e
                for c in range(NC_F):
                    cg = c // 2
                    gidx = eg * NGRP + cg
                    par = gidx % 3
                    if c % 2 == 0:
                        if gidx < NGRP * E:
                            nc.tensor.wait_ge(s_uwld[gidx % 4].h, 32 * (gidx // 4 + 1))
                        else:
                            nhw = (gidx - NGRP * E) // 4 + 1
                            nc.tensor.wait_ge(s_uwldh[gidx % 4].h, 32 * nhw)
                    if eg == 0 and c == 0:
                        nc.tensor.wait_ge(s_xnT.h, 4)
                    if c >= 2:
                        nc.tensor.wait_ge(s_sl.h, eg * NC_F + c - 1)
                        nc.tensor.wait_ge(s_h.h, eg * NC_F + c - 1)
                    elif eg >= 1:
                        nc.tensor.wait_ge(s_sl.h, (eg - 1) * NC_F + NC_F - 2 + c + 1)
                        nc.tensor.wait_ge(s_h.h, (eg - 1) * NC_F + NC_F - 2 + c + 1)
                        nc.tensor.wait_ge(s_gacc.h, 8 * (eg - 1) + 2 * c + 1)
                    ba = bank(2 * (c % 2))
                    bu = bank(2 * (c % 2) + 1)
                    ci = c % G
                    for jc in range(JC):
                        inst = nc.tensor.matmul(
                            ba, wg_buf(gidx % 4)[:, jc, 128 * ci:128 * (ci + 1)],
                            xnT[:, jc, tok0:tok0 + T_PASS],
                            start=(jc == 0), stop=(jc == JC - 1))
                    s_ups.inc(inst)
                    if eg >= 1 and c < 2:
                        nc.tensor.wait_ge(s_gacc.h, 8 * (eg - 1) + 2 * c + 2)
                    for jc in range(JC):
                        inst = nc.tensor.matmul(
                            bu, wu_buf(gidx % 4)[:, jc, 128 * ci:128 * (ci + 1)],
                            xnT[:, jc, tok0:tok0 + T_PASS],
                            start=(jc == 0), stop=(jc == JC - 1))
                    s_ups.inc(inst)
                nc.tensor.wait_ge(s_h.h, (eg + 1) * NC_F)
                if eg == 0:
                    nc.tensor.wait_ge(s_xnT.h, NT)
                for c in range(NC_F):
                    cg = c // 2
                    gidx = eg * NGRP + cg
                    par = gidx % 3
                    if c % 2 == 0:
                        nc.tensor.wait_ge(s_dwld[gidx % 3].h, 16 * (gidx // 3 + 1))
                    for grp in range(8):
                        t, dh = grp // 2, grp % 2
                        if c == 0 and eg >= 1:
                            nc.tensor.wait_ge(s_gacc.h, 8 * (eg - 1) + grp + 1)
                        inst = nc.tensor.matmul(
                            bank(grp), H_sb[:, c, 128 * t:128 * (t + 1)],
                            wd_buf(par)[:, c % G, 512 * dh:512 * (dh + 1)],
                            start=(c == 0), stop=(c == NC_F - 1))
                        if c == NC_F - 1:
                            s_dps.inc(inst)
                        if grp == 7 and c % 2 == 1 and c != NC_F - 1:
                            s_dcons.inc(inst)
                for c in range(NC_F):
                    nc.scalar.wait_ge(s_ups.h, eg * 2 * NC_F + 2 * c + 1)
                    if eg * NC_F + c >= 2:
                        nc.scalar.wait_ge(s_h.h, eg * NC_F + c - 1)
                    s_sl.inc(nc.scalar.activation(tmp_sl[:, c % 2, 0:T_PASS], bank(2 * (c % 2)),
                                                  AF.Silu))
                for c in range(NC_F):
                    nc.vector.wait_ge(s_ups.h, eg * 2 * NC_F + 2 * c + 2)
                    nc.vector.wait_ge(s_sl.h, eg * NC_F + c + 1)
                    if c == 0 and eg >= 1:
                        nc.vector.wait_ge(s_dps.h, 8 * eg)
                    s_h.inc(nc.vector.tensor_tensor(H_sb[:, c, :], tmp_sl[:, c % 2, 0:T_PASS],
                                                    bank(2 * (c % 2) + 1), op=ALU.mult))
                for grp in range(8):
                    t, dh = grp // 2, grp % 2
                    nc.vector.wait_ge(s_dps.h, eg * 8 + grp + 1)
                    dst = g_sb[:, 4 * pss + t, 512 * dh:512 * (dh + 1)]
                    if e == 0:
                        s_gacc.inc(nc.vector.tensor_copy(dst, bank(grp)))
                    else:
                        nc.vector.wait_ge(s_gacc.h, 8 * (eg - 1) + grp + 1)
                        s_gacc.inc(nc.vector.tensor_tensor(dst, dst, bank(grp), op=ALU.add))

        es_ffn.close()
        es_w2.close()
        vtok = es.enter_context(nc.sbuf_tensor("vtok", [128, NT, D], BF16))
        att_sb = es.enter_context(nc.sbuf_tensor("att_sb", [128, 2, D], BF16))
        ptab = es.enter_context(nc.sbuf_tensor("ptab", [128, NH * NDELTA, 128], BF16))

        nc.scalar.wait_ge(s_dps.h, NPASS * E * 8)
        for h in range(NH):
            std = ATTN_STD[h]
            inv_norm = 1.0 / (std * np.sqrt(2.0 * np.pi))
            for di in range(NDELTA):
                if s_pt.n:
                    nc.scalar.wait_ge(s_pt.h, s_pt.n)
                s_pt.inc(nc.scalar.activation(ptmp[:, :], io_f[:, di, :], AF.Square,
                                              bias=1.0, scale=1.0 / std))
                nc.scalar.wait_ge(s_pt.h, s_pt.n)
                s_pt.inc(nc.scalar.activation(ptmp[:, :], ptmp[:, :], AF.Exp, scale=-0.5))
                nc.scalar.wait_ge(s_pt.h, s_pt.n)
                s_pt.inc(nc.scalar.mul(ptab[:, h * NDELTA + di, :], ptmp[:, :], inv_norm))
        assert s_pt.n == PT_TOTAL
        es_wiT = es.enter_context(ExitStack())
        wiT = es_wiT.enter_context(nc.sbuf_tensor("wiT", [128, JC, D], BF16))

        def wio_prep(ic, dst):
            nc.tensor.wait_ge(s_wio[ic % 2].h, 16 * (ic // 2 + 1))
            if ic >= 2:
                nc.tensor.wait_ge(s_wiot.h, ic - 1)
            tb = bank_bf(6 + ic % 2)
            for jc in range(JC):
                inst = nc.tensor.transpose(tb[:, 128 * jc:128 * (jc + 1)],
                                           wio_sb[:, ic % 2, 128 * jc:128 * (jc + 1)],
                                           ident[:, :])
            s_wiop.inc(inst)
            nc.vector.wait_ge(s_wiop.h, ic + 1)
            dstT = dst[:, :, 128 * (ic % 8):128 * (ic % 8 + 1)]
            s_wiot.inc(nc.vector.tensor_copy(dstT, tb.rearrange("p (a f) -> p a f", a=JC)))

        nc.tensor.wait_ge(s_gacc.h, s_gacc.n)
        for ic in range(8):
            wio_prep(ic, wiT)

        for lt in range(NT):
            if lt == 0:
                nc.tensor.wait_ge(s_wiot.h, 8)
            if lt >= 2:
                nc.tensor.wait_ge(s_vt.h, lt - 1)
            for ih in range(2):
                vb = bank(2 * (lt % 2) + ih)
                for jc in range(JC):
                    inst = nc.tensor.matmul(
                        vb, xnT[:, jc, 128 * lt:128 * (lt + 1)],
                        wiT[:, jc, 512 * ih:512 * (ih + 1)],
                        start=(jc == 0), stop=(jc == JC - 1))
            s_vps.inc(inst)
            nc.vector.wait_ge(s_vps.h, lt + 1)
            s_vt.inc(nc.vector.tensor_copy(vtok[:, lt, :], bank(2 * (lt % 2), 1024)))

        es_wiT.close()
        es_xnT.close()
        attT = es.enter_context(nc.sbuf_tensor("attT", [128, JC, R], BF16))
        woT = es.enter_context(nc.sbuf_tensor("woT", [128, JC, D], BF16))

        for ic in range(8, 16):
            wio_prep(ic, woT)

        for m in range(NM + 1):
            if m < NM:
                if m == 0:
                    nc.tensor.wait_ge(s_vt.h, NT)
                    nc.tensor.wait_ge(s_pt.h, PT_TOTAL)
                if m >= 2:
                    nc.tensor.wait_ge(s_att.h, m - 1)
                ab = bank(2 * (m % 2), 1024)
                for h in range(NH):
                    for di in range(NDELTA):
                        inst = nc.tensor.matmul(
                            ab[:, 128 * h:128 * (h + 1)],
                            ptab[:, h * NDELTA + di, :],
                            vtok[:, m + di, 128 * h:128 * (h + 1)],
                            start=(di == 0), stop=(di == NDELTA - 1))
                s_aps.inc(inst)
                nc.vector.wait_ge(s_aps.h, m + 1)
                s_att.inc(nc.vector.tensor_copy(att_sb[:, m % 2, :], ab))
            if m >= 1:
                mm = m - 1
                nc.tensor.wait_ge(s_att.h, mm + 1)
                if mm >= 2:
                    nc.tensor.wait_ge(s_attT.h, mm - 1)
                tb = bank_bf(4 + mm % 2)
                for jc in range(JC):
                    inst = nc.tensor.transpose(tb[:, 128 * jc:128 * (jc + 1)],
                                               att_sb[:, mm % 2, 128 * jc:128 * (jc + 1)],
                                               ident[:, :])
                s_atp.inc(inst)
                nc.vector.wait_ge(s_atp.h, mm + 1)
                dstT = attT[:, :, 128 * mm:128 * (mm + 1)]
                s_attT.inc(nc.vector.tensor_copy(dstT, tb.rearrange("p (a f) -> p a f", a=JC)))

        for m in range(NM):
            if m == 0:
                nc.tensor.wait_ge(s_att.h, NM)
                nc.tensor.wait_ge(s_wiot.h, 16)
            nc.tensor.wait_ge(s_attT.h, m + 1)
            if m >= 2:
                nc.tensor.wait_ge(s_fv.h, 2 * (m - 2) + 1)
            for ih in range(2):
                sb = bank(2 * (m % 2) + ih)
                for jc in range(JC):
                    inst = nc.tensor.matmul(
                        sb, attT[:, jc, 128 * m:128 * (m + 1)],
                        woT[:, jc, 512 * ih:512 * (ih + 1)],
                        start=(jc == 0), stop=(jc == JC - 1))
            s_sps.inc(inst)
            if m == 0:
                nc.scalar.wait_ge(s_p0v.h, 5 * NT)
            nc.scalar.wait_ge(s_gacc.h, 8 * E * (m // 4 + 1))
            if m >= 2:
                nc.scalar.wait_ge(s_fv.h, 2 * (m - 2) + 2)
            s_fs.inc(nc.scalar.activation(xc[:, m % 2, :], g_sb[:, m, :], AF.Silu))
            nc.vector.wait_ge(s_sps.h, m + 1)
            nc.vector.wait_ge(s_fs.h, m + 1)
            if m >= 2:
                nc.vector.wait_ge(s_out[m % 2].h, 16 * (m // 2))
            s_fv.inc(nc.vector.tensor_tensor(xc[:, m % 2, :], bank(2 * (m % 2), 1024),
                                             xc[:, m % 2, :], op=ALU.mult))
            nc.vector.wait_ge(s_fv.h, 2 * m + 1)
            nc.vector.wait_ge(s_x[m % 2].h, x_base[m % 2] + 16 * (m // 2 + 1))
            s_fv.inc(nc.vector.tensor_tensor(x_t[:, m % 2, :], xc[:, m % 2, :],
                                             x_t[:, m % 2, :], op=ALU.add))

        if DEBUG:
            nc.sync.wait_ge(s_gacc.h, NPASS * E * 8)
            s_out[0].inc(nc.sync.dma_start(dbg_g[:, :, :], g_sb[:, :, :]), 16)
            nc.sync.wait_ge(s_vt.h, NT)
            s_out[0].inc(nc.sync.dma_start(dbg_vtok[:, :, :], vtok[:, :, :]), 16)
            nc.sync.wait_ge(s_attT.h, NM)
            s_out[0].inc(nc.sync.dma_start(dbg_attT[:, :, :], attT[:, :, :]), 16)
        nc.sync.wait_ge(s_out[0].h, s_out[0].n)
        nc.sync.wait_ge(s_out[1].h, s_out[1].n)

    nc.finalize()
    return nc


def kernel(L0_attn_values, L0_attn_input_weights, L0_attn_output_weight,
           L0_ffn_Wg, L0_ffn_Wu, L0_ffn_Wd):
    x = np.ascontiguousarray(np.asarray(L0_attn_values, dtype=np.float32))
    wi = np.ascontiguousarray(np.asarray(L0_attn_input_weights, dtype=np.float32))
    wo = np.ascontiguousarray(np.asarray(L0_attn_output_weight, dtype=np.float32))
    wg = np.ascontiguousarray(np.asarray(L0_ffn_Wg, dtype=np.float32))
    wu = np.ascontiguousarray(np.asarray(L0_ffn_Wu, dtype=np.float32))
    wd = np.ascontiguousarray(np.asarray(L0_ffn_Wd, dtype=np.float32))
    B, L, _ = x.shape

    nc = build_program()
    in_maps = []
    for c in range(8):
        b, s0 = c // 4, (c % 4) * R
        xs = np.zeros((RT, D), dtype=np.float32)
        lo = max(0, s0 - HALO_L)
        xs[HALO_L - (s0 - lo):HALO_L] = x[b, lo:s0]
        hi = min(L, s0 + R + HALO_R)
        xs[HALO_L:HALO_L + (hi - s0)] = x[b, s0:hi]
        in_maps.append({"x": xs, "wi": wi, "wo": wo, "wg": wg, "wu": wu, "wd": wd})

    res = run_bass_kernel_spmd(nc, in_maps, core_ids=list(range(8)))
    out = np.empty((B, L, D), dtype=np.float32)
    for c in range(8):
        b, s0 = c // 4, (c % 4) * R
        out[b, s0:s0 + R] = res.results[c]["out"]
    return out


# version 19
# speedup vs baseline: 1.1248x; 1.1248x over previous
import numpy as np
from contextlib import ExitStack

import concourse.bass as bass
import concourse.mybir as mybir
from concourse.bass_utils import run_bass_kernel_spmd

F32 = mybir.dt.float32
BF16 = mybir.dt.bfloat16
I32 = mybir.dt.int32
AF = mybir.ActivationFunctionType
ALU = mybir.AluOpType
AX = mybir.AxisListType

D = 1024
JC = 8
F = 4096
NC_F = 32
E = 8
R = 2048
HALO_L, HALO_R = 256, 128
RT = HALO_L + R + HALO_R
NT = RT // 128
NM = R // 128
T_PASS = 512
NPASS = R // T_PASS
G = 2
NGRP = NC_F // G
NH = 8
ATTN_STD = [1.0, 2.0, 4.0, 8.0, 1.0, 2.0, 4.0, 8.0]
ATTN_OFF = [-1.0, -2.0, -4.0, -8.0, -1.0, -2.0, -4.0, -8.0]
NDELTA = 4

DEBUG = False


class Sem:
    def __init__(self, es, nc, name):
        self.h = es.enter_context(nc.semaphore(name))
        self.n = 0

    def inc(self, inst, k=1):
        inst.then_inc(self.h, k)
        self.n += k
        return self.n


def build_program():
    nc = bass.Bass("TRN2", target_bir_lowering=False)

    x_h = nc.dram_tensor("x", [RT, D], F32, kind="ExternalInput")
    wi_h = nc.dram_tensor("wi", [D, D], F32, kind="ExternalInput")
    wo_h = nc.dram_tensor("wo", [D, D], F32, kind="ExternalInput")
    wg_h = nc.dram_tensor("wg", [E, D, F], F32, kind="ExternalInput")
    wu_h = nc.dram_tensor("wu", [E, D, F], F32, kind="ExternalInput")
    wd_h = nc.dram_tensor("wd", [E, F, D], F32, kind="ExternalInput")
    out_h = nc.dram_tensor("out", [R, D], F32, kind="ExternalOutput")
    wgs_h = nc.dram_tensor("wgs", [E, NGRP, 128, JC * G * 128], BF16, kind="Internal")
    wus_h = nc.dram_tensor("wus", [E, NGRP, 128, JC * G * 128], BF16, kind="Internal")
    if DEBUG:
        dbg_vtok = nc.dram_tensor("dbg_vtok", [128, NT, D], BF16, kind="ExternalOutput")
        dbg_attT = nc.dram_tensor("dbg_attT", [128, JC, R], BF16, kind="ExternalOutput")
        dbg_g = nc.dram_tensor("dbg_g", [128, NM, D], F32, kind="ExternalOutput")
        dbg_H = nc.dram_tensor("dbg_H", [128, NC_F, T_PASS], BF16, kind="ExternalOutput")
        dbg_xnT2 = nc.dram_tensor("dbg_xnT2", [128, JC, RT], BF16, kind="ExternalOutput")

    with ExitStack() as es:
        g_sb = es.enter_context(nc.sbuf_tensor("g_sb", [128, NM, D], F32))
        x_t = es.enter_context(nc.sbuf_tensor("x_t", [128, 2, D], F32))
        xc = es.enter_context(nc.sbuf_tensor("xc", [128, 2, D], F32))
        stats = es.enter_context(nc.sbuf_tensor("stats", [128, 2, 16], F32))
        ident = es.enter_context(nc.sbuf_tensor("ident", [128, 128], BF16))
        io_f = es.enter_context(nc.sbuf_tensor("io_f", [128, NDELTA, 128], F32, side="right"))
        eps_c = es.enter_context(nc.sbuf_tensor("eps_c", [128, 1], F32))
        es_xnT = es.enter_context(ExitStack())
        xnT = es_xnT.enter_context(nc.sbuf_tensor("xnT", [128, JC, RT], BF16, side="right"))
        es_w2 = ExitStack()
        wgb2 = es_w2.enter_context(nc.sbuf_tensor("wgb2", [128, 2, JC, G * 128], BF16, side="right"))
        wub2 = es_w2.enter_context(nc.sbuf_tensor("wub2", [128, 2, JC, G * 128], BF16, side="right"))
        wio_sb = es.enter_context(nc.sbuf_tensor("wio_sb", [128, 2, D], BF16))
        ptmp = es.enter_context(nc.sbuf_tensor("ptmp", [128, 128], F32))
        es_xnb = ExitStack()
        xn_b = es_xnb.enter_context(nc.sbuf_tensor("xn_b", [128, 2, D], BF16))
        es_io = ExitStack()
        io_i = es_io.enter_context(nc.sbuf_tensor("io_i", [128, NDELTA, 128], I32))
        es_ffn = es.enter_context(ExitStack())

        psum = es.enter_context(nc.psum_tensor("psum", [128, 4096], F32))

        def bank(i, w=512):
            return psum[:, 512 * i: 512 * i + w]

        psum_bf = psum[:, :].bitcast(BF16)

        def bank_bf(i):
            return psum_bf[:, 1024 * i: 1024 * (i + 1)]

        s_gp = Sem(es, nc, "s_gp")
        s_x = [Sem(es, nc, "s_x0"), Sem(es, nc, "s_x1")]
        s_p0v = Sem(es, nc, "s_p0v")
        s_p0s = Sem(es, nc, "s_p0s")
        s_tp = Sem(es, nc, "s_tp")
        s_xnT = Sem(es, nc, "s_xnT")
        s_pt = Sem(es, nc, "s_pt")
        s_uwld = [Sem(es, nc, f"s_uwld{i}") for i in range(4)]
        s_uwldh = [Sem(es, nc, f"s_uwldh{i}") for i in range(4)]
        s_dwld = [Sem(es, nc, f"s_dwld{i}") for i in range(4)]
        s_dcons = Sem(es, nc, "s_dcons")
        s_wst_u = [Sem(es, nc, f"s_wst_u{i}") for i in range(4)]
        s_wst_d = [Sem(es, nc, f"s_wst_d{i}") for i in range(3)]
        s_ups = Sem(es, nc, "s_ups")
        s_sl = Sem(es, nc, "s_sl")
        s_h = Sem(es, nc, "s_h")
        s_dps = Sem(es, nc, "s_dps")
        s_gacc = Sem(es, nc, "s_gacc")
        s_wio = [Sem(es, nc, "s_wio0"), Sem(es, nc, "s_wio1")]
        s_wiop = Sem(es, nc, "s_wiop")
        s_wiot = Sem(es, nc, "s_wiot")
        s_vps = Sem(es, nc, "s_vps")
        s_vt = Sem(es, nc, "s_vt")
        s_aps = Sem(es, nc, "s_aps")
        s_att = Sem(es, nc, "s_att")
        s_atp = Sem(es, nc, "s_atp")
        s_attT = Sem(es, nc, "s_attT")
        s_sps = Sem(es, nc, "s_sps")
        s_fs = Sem(es, nc, "s_fs")
        s_fv = Sem(es, nc, "s_fv")
        s_out = [Sem(es, nc, "s_out0"), Sem(es, nc, "s_out1")]

        s_gp.inc(nc.gpsimd.memset(ident[:, :], 0.0))
        nc.gpsimd.wait_ge(s_gp.h, 1)
        s_gp.inc(nc.gpsimd.affine_select(
            out=ident[:, :], in_=ident[:, :], compare_op=ALU.not_equal,
            fill=1.0, base=0, pattern=[[-1, 128]], channel_multiplier=1))
        for di in range(NDELTA):
            s_gp.inc(nc.gpsimd.iota(io_i[:, di, :], pattern=[[-1, 128]], base=128 * (di - 2),
                                    channel_multiplier=1))
        nc.gpsimd.wait_ge(s_gp.h, 6)
        s_gp.inc(nc.gpsimd.tensor_copy(io_f[:, :, :], io_i[:, :, :]))
        s_gp.inc(nc.gpsimd.memset(eps_c[:, :], 1e-5))
        es_io.close()

        P0_ORDER_SYNC = [2, 3, 4, 5, 0, 1] + list(range(6, NT))
        for pos, lt in enumerate(P0_ORDER_SYNC):
            if pos >= 2:
                nc.sync.wait_ge(s_p0v.h, 5 * (pos - 2) + 5)
            s_x[pos % 2].inc(nc.sync.dma_start(x_t[:, pos % 2, :], x_h[128 * lt:128 * (lt + 1), :]), 16)
        if DEBUG:
            nc.sync.wait_ge(s_xnT.h, NT)
            s_out[0].inc(nc.sync.dma_start(dbg_xnT2[:, :, :], xnT[:, :, :]), 16)

        P0_ORDER = [2, 3, 4, 5, 0, 1] + list(range(6, NT))
        for pos, lt in enumerate(P0_ORDER):
            p = pos % 2
            nc.vector.wait_ge(s_x[p].h, 16 * (pos // 2 + 1))
            s_p0v.inc(nc.vector.bn_stats(stats[:, p, 4:10], x_t[:, p, 0:512]))
            s_p0v.inc(nc.vector.bn_stats(stats[:, p, 10:16], x_t[:, p, 512:1024]))
            nc.vector.wait_ge(s_p0v.h, 5 * pos + 2)
            s_p0v.inc(nc.vector.bn_aggr(stats[:, p, 0:2], stats[:, p, 4:16]))
            nc.vector.wait_ge(s_p0s.h, pos + 1)
            s_p0v.inc(nc.vector.reciprocal(stats[:, p, 3:4], stats[:, p, 2:3]))
            nc.vector.wait_ge(s_p0v.h, 5 * pos + 4)
            if pos >= 2:
                nc.vector.wait_ge(s_tp.h, pos - 1)
            s_p0v.inc(nc.vector.tensor_scalar(xn_b[:, p, :], x_t[:, p, :],
                                              stats[:, p, 0:1], stats[:, p, 3:4],
                                              op0=ALU.subtract, op1=ALU.mult))
            if pos == 0:
                nc.scalar.wait_ge(s_gp.h, 8)
            nc.scalar.wait_ge(s_p0v.h, 5 * pos + 3)
            s_p0s.inc(nc.scalar.activation(stats[:, p, 2:3], stats[:, p, 1:2], AF.Sqrt,
                                           bias=eps_c[:, 0:1], scale=1.0))
            nc.tensor.wait_ge(s_p0v.h, 5 * pos + 5)
            if pos == 0:
                nc.tensor.wait_ge(s_gp.h, 2)
            if pos >= 2:
                nc.tensor.wait_ge(s_xnT.h, pos - 1)
            tb = bank_bf(4 + p)
            for jc in range(JC):
                inst = nc.tensor.transpose(tb[:, 128 * jc:128 * (jc + 1)],
                                           xn_b[:, p, 128 * jc:128 * (jc + 1)], ident[:, :])
            s_tp.inc(inst)
            nc.vector.wait_ge(s_tp.h, pos + 1)
            dst = xnT[:, :, 128 * lt:128 * (lt + 1)]
            srcv = tb.rearrange("p (a f) -> p a f", a=JC)
            s_xnT.inc(nc.vector.tensor_copy(dst, srcv))

        PT_TOTAL = NH * NDELTA * 3


        es_xnb.close()
        wdb2 = es_ffn.enter_context(nc.sbuf_tensor("wdb2", [128, 2, G, D], BF16))
        wgb = es_ffn.enter_context(nc.sbuf_tensor("wgb", [128, 2, JC, G * 128], BF16))
        wub = es_ffn.enter_context(nc.sbuf_tensor("wub", [128, 2, JC, G * 128], BF16))
        wdb = es_ffn.enter_context(nc.sbuf_tensor("wdb", [128, 2, G, D], BF16))

        def wg_buf(par):
            return wgb[:, par, :, :] if par < 2 else wgb2[:, par - 2, :, :]

        def wu_buf(par):
            return wub[:, par, :, :] if par < 2 else wub2[:, par - 2, :, :]

        def wd_buf(par):
            return wdb[:, par, :, :] if par < 2 else wdb2[:, par - 2, :, :]

        NEG = NPASS * E
        issue = [("U", 0, cg) for cg in range(NGRP)]
        for eg in range(NEG):
            issue += [("D", eg, cg) for cg in range(NGRP - 2)]
            if eg + 1 < NEG:
                issue += [("U", eg + 1, 0), ("U", eg + 1, 1)]
            issue += [("D", eg, NGRP - 2), ("D", eg, NGRP - 1)]
            if eg + 1 < NEG:
                issue += [("U", eg + 1, cg) for cg in range(2, NGRP)]
        assert len(issue) == 2 * NEG * NGRP

        nc.gpsimd.wait_ge(s_gp.h, 8)
        first_d2 = True
        for kind, eg, cg in issue:
            if eg >= E and kind == "U":
                continue
            e = eg % E
            gidx = eg * NGRP + cg
            par = gidx % 3
            upar = gidx % 4
            c0 = cg * G * 128
            if kind == "U":
                if gidx >= 4:
                    ep, k = divmod(gidx - 4, NGRP)
                    nc.gpsimd.wait_ge(s_ups.h, ep * 2 * NC_F + 4 * k + 4)
                    nc.gpsimd.wait_ge(s_wst_u[upar].h, 32 * (gidx // 4))
                src = wg_h[e, :, c0:c0 + G * 128].rearrange("(a p) f -> p a f", p=128)
                s_uwld[upar].inc(nc.gpsimd.dma_start(wg_buf(upar), src), 16)
                src = wu_h[e, :, c0:c0 + G * 128].rearrange("(a p) f -> p a f", p=128)
                s_uwld[upar].inc(nc.gpsimd.dma_start(wu_buf(upar), src), 16)
            else:
                dpar = gidx % 4
                if dpar >= 2 and first_d2:
                    nc.gpsimd.wait_ge(s_xnT.h, NT)
                    first_d2 = False
                if gidx >= 4:
                    ep, k = divmod(gidx - 4, NGRP)
                    if k < NGRP - 1:
                        nc.gpsimd.wait_ge(s_dcons.h, ep * (NGRP - 1) + k + 1)
                    else:
                        nc.gpsimd.wait_ge(s_dps.h, 8 * (ep + 1))
                src = wd_h[e, c0:c0 + G * 128, :].rearrange("(a p) d -> p a d", p=128)
                s_dwld[dpar].inc(nc.gpsimd.dma_start(wd_buf(dpar), src), 16)

        def wg_flat(par):
            return wg_buf(par).rearrange("p b f -> p (b f)")

        def wu_flat(par):
            return wu_buf(par).rearrange("p b f -> p (b f)")

        def wd_flat(par):
            return wd_buf(par).rearrange("p b f -> p (b f)")

        for kind, eg, cg in issue:
            e = eg % E
            gidx = eg * NGRP + cg
            upar = gidx % 4
            if kind != "U":
                continue
            if eg < E:
                nc.sync.wait_ge(s_uwld[upar].h, 32 * (gidx // 4 + 1))
                s_wst_u[upar].inc(nc.sync.dma_start(wgs_h[e, cg], wg_flat(upar)), 16)
                s_wst_u[upar].inc(nc.sync.dma_start(wus_h[e, cg], wu_flat(upar)), 16)
            else:
                ep, k = divmod(gidx - 4, NGRP)
                nc.sync.wait_ge(s_ups.h, ep * 2 * NC_F + 4 * k + 4)
                if gidx < NGRP * E + 4:
                    nc.sync.wait_ge(s_wst_u[upar].h, 32 * (gidx // 4))
                s_uwldh[upar].inc(nc.sync.dma_start(wg_flat(upar), wgs_h[e, cg]), 16)
                s_uwldh[upar].inc(nc.sync.dma_start(wu_flat(upar), wus_h[e, cg]), 16)

        x_base = [s_x[0].n, s_x[1].n]
        for m in range(NM):
            if m >= 2:
                nc.sync.wait_ge(s_fv.h, 2 * (m - 2) + 2)
                nc.sync.wait_ge(s_out[m % 2].h, 16 * (m // 2))
            else:
                last_pos = NT - 1 - ((NT - 1 + m) % 2)
                nc.sync.wait_ge(s_p0v.h, 5 * last_pos + 5)
            r0 = HALO_L + 128 * m
            s_x[m % 2].inc(nc.sync.dma_start(x_t[:, m % 2, :], x_h[r0:r0 + 128, :]), 16)
            nc.sync.wait_ge(s_fv.h, 2 * m + 2)
            s_out[m % 2].inc(nc.sync.dma_start(out_h[128 * m:128 * (m + 1), :], x_t[:, m % 2, :]), 16)

        for ic in range(16):
            if ic >= 2:
                nc.gpsimd.wait_ge(s_wiop.h, ic - 1)
            src_h = wi_h if ic < 8 else wo_h
            r0 = (ic % 8) * 128
            s_wio[ic % 2].inc(nc.gpsimd.dma_start(wio_sb[:, ic % 2, :], src_h[r0:r0 + 128, :]), 16)


        H_sb = es_ffn.enter_context(nc.sbuf_tensor("H_sb", [128, NC_F, T_PASS], BF16))
        tmp_sl = xc
        if DEBUG:
            nc.gpsimd.wait_ge(s_h.h, NPASS * E * NC_F)
            s_out[0].inc(nc.gpsimd.dma_start(dbg_H[:, :, :], H_sb[:, :, :]), 16)

        for pss in range(NPASS):
            tok0 = HALO_L + T_PASS * pss
            for e in range(E):
                eg = pss * E + e
                for c in range(NC_F):
                    cg = c // 2
                    gidx = eg * NGRP + cg
                    par = gidx % 3
                    if c % 2 == 0:
                        if gidx < NGRP * E:
                            nc.tensor.wait_ge(s_uwld[gidx % 4].h, 32 * (gidx // 4 + 1))
                        else:
                            nhw = (gidx - NGRP * E) // 4 + 1
                            nc.tensor.wait_ge(s_uwldh[gidx % 4].h, 32 * nhw)
                    if eg == 0 and c == 0:
                        nc.tensor.wait_ge(s_xnT.h, 4)
                    if c >= 2:
                        nc.tensor.wait_ge(s_sl.h, eg * NC_F + c - 1)
                        nc.tensor.wait_ge(s_h.h, eg * NC_F + c - 1)
                    elif eg >= 1:
                        nc.tensor.wait_ge(s_sl.h, (eg - 1) * NC_F + NC_F - 2 + c + 1)
                        nc.tensor.wait_ge(s_h.h, (eg - 1) * NC_F + NC_F - 2 + c + 1)
                        nc.tensor.wait_ge(s_gacc.h, 8 * (eg - 1) + 2 * c + 1)
                    ba = bank(2 * (c % 2))
                    bu = bank(2 * (c % 2) + 1)
                    ci = c % G
                    for jc in range(JC):
                        inst = nc.tensor.matmul(
                            ba, wg_buf(gidx % 4)[:, jc, 128 * ci:128 * (ci + 1)],
                            xnT[:, jc, tok0:tok0 + T_PASS],
                            start=(jc == 0), stop=(jc == JC - 1))
                    s_ups.inc(inst)
                    if eg >= 1 and c < 2:
                        nc.tensor.wait_ge(s_gacc.h, 8 * (eg - 1) + 2 * c + 2)
                    for jc in range(JC):
                        inst = nc.tensor.matmul(
                            bu, wu_buf(gidx % 4)[:, jc, 128 * ci:128 * (ci + 1)],
                            xnT[:, jc, tok0:tok0 + T_PASS],
                            start=(jc == 0), stop=(jc == JC - 1))
                    s_ups.inc(inst)
                nc.tensor.wait_ge(s_h.h, (eg + 1) * NC_F)
                if eg == 0:
                    nc.tensor.wait_ge(s_xnT.h, NT)
                for c in range(NC_F):
                    cg = c // 2
                    gidx = eg * NGRP + cg
                    par = gidx % 4
                    if c % 2 == 0:
                        nc.tensor.wait_ge(s_dwld[gidx % 4].h, 16 * (gidx // 4 + 1))
                    for grp in range(8):
                        t, dh = grp // 2, grp % 2
                        if c == 0 and eg >= 1:
                            nc.tensor.wait_ge(s_gacc.h, 8 * (eg - 1) + grp + 1)
                        inst = nc.tensor.matmul(
                            bank(grp), H_sb[:, c, 128 * t:128 * (t + 1)],
                            wd_buf(par)[:, c % G, 512 * dh:512 * (dh + 1)],
                            start=(c == 0), stop=(c == NC_F - 1))
                        if c == NC_F - 1:
                            s_dps.inc(inst)
                        if grp == 7 and c % 2 == 1 and c != NC_F - 1:
                            s_dcons.inc(inst)
                for c in range(NC_F):
                    nc.scalar.wait_ge(s_ups.h, eg * 2 * NC_F + 2 * c + 1)
                    if eg * NC_F + c >= 2:
                        nc.scalar.wait_ge(s_h.h, eg * NC_F + c - 1)
                    s_sl.inc(nc.scalar.activation(tmp_sl[:, c % 2, 0:T_PASS], bank(2 * (c % 2)),
                                                  AF.Silu))
                for c in range(NC_F):
                    nc.vector.wait_ge(s_ups.h, eg * 2 * NC_F + 2 * c + 2)
                    nc.vector.wait_ge(s_sl.h, eg * NC_F + c + 1)
                    if c == 0 and eg >= 1:
                        nc.vector.wait_ge(s_dps.h, 8 * eg)
                    s_h.inc(nc.vector.tensor_tensor(H_sb[:, c, :], tmp_sl[:, c % 2, 0:T_PASS],
                                                    bank(2 * (c % 2) + 1), op=ALU.mult))
                for grp in range(8):
                    t, dh = grp // 2, grp % 2
                    nc.vector.wait_ge(s_dps.h, eg * 8 + grp + 1)
                    dst = g_sb[:, 4 * pss + t, 512 * dh:512 * (dh + 1)]
                    if e == 0:
                        s_gacc.inc(nc.vector.tensor_copy(dst, bank(grp)))
                    else:
                        nc.vector.wait_ge(s_gacc.h, 8 * (eg - 1) + grp + 1)
                        s_gacc.inc(nc.vector.tensor_tensor(dst, dst, bank(grp), op=ALU.add))

        es_ffn.close()
        es_w2.close()
        vtok = es.enter_context(nc.sbuf_tensor("vtok", [128, NT, D], BF16))
        att_sb = es.enter_context(nc.sbuf_tensor("att_sb", [128, 2, D], BF16))
        ptab = es.enter_context(nc.sbuf_tensor("ptab", [128, NH * NDELTA, 128], BF16))

        nc.scalar.wait_ge(s_dps.h, NPASS * E * 8)
        for h in range(NH):
            std = ATTN_STD[h]
            inv_norm = 1.0 / (std * np.sqrt(2.0 * np.pi))
            for di in range(NDELTA):
                if s_pt.n:
                    nc.scalar.wait_ge(s_pt.h, s_pt.n)
                s_pt.inc(nc.scalar.activation(ptmp[:, :], io_f[:, di, :], AF.Square,
                                              bias=1.0, scale=1.0 / std))
                nc.scalar.wait_ge(s_pt.h, s_pt.n)
                s_pt.inc(nc.scalar.activation(ptmp[:, :], ptmp[:, :], AF.Exp, scale=-0.5))
                nc.scalar.wait_ge(s_pt.h, s_pt.n)
                s_pt.inc(nc.scalar.mul(ptab[:, h * NDELTA + di, :], ptmp[:, :], inv_norm))
        assert s_pt.n == PT_TOTAL
        es_wiT = es.enter_context(ExitStack())
        wiT = es_wiT.enter_context(nc.sbuf_tensor("wiT", [128, JC, D], BF16))

        def wio_prep(ic, dst):
            nc.tensor.wait_ge(s_wio[ic % 2].h, 16 * (ic // 2 + 1))
            if ic >= 2:
                nc.tensor.wait_ge(s_wiot.h, ic - 1)
            tb = bank_bf(6 + ic % 2)
            for jc in range(JC):
                inst = nc.tensor.transpose(tb[:, 128 * jc:128 * (jc + 1)],
                                           wio_sb[:, ic % 2, 128 * jc:128 * (jc + 1)],
                                           ident[:, :])
            s_wiop.inc(inst)
            nc.vector.wait_ge(s_wiop.h, ic + 1)
            dstT = dst[:, :, 128 * (ic % 8):128 * (ic % 8 + 1)]
            s_wiot.inc(nc.vector.tensor_copy(dstT, tb.rearrange("p (a f) -> p a f", a=JC)))

        nc.tensor.wait_ge(s_gacc.h, s_gacc.n)
        for ic in range(8):
            wio_prep(ic, wiT)

        for lt in range(NT):
            if lt == 0:
                nc.tensor.wait_ge(s_wiot.h, 8)
            if lt >= 2:
                nc.tensor.wait_ge(s_vt.h, lt - 1)
            for ih in range(2):
                vb = bank(2 * (lt % 2) + ih)
                for jc in range(JC):
                    inst = nc.tensor.matmul(
                        vb, xnT[:, jc, 128 * lt:128 * (lt + 1)],
                        wiT[:, jc, 512 * ih:512 * (ih + 1)],
                        start=(jc == 0), stop=(jc == JC - 1))
            s_vps.inc(inst)
            nc.vector.wait_ge(s_vps.h, lt + 1)
            s_vt.inc(nc.vector.tensor_copy(vtok[:, lt, :], bank(2 * (lt % 2), 1024)))

        es_wiT.close()
        es_xnT.close()
        attT = es.enter_context(nc.sbuf_tensor("attT", [128, JC, R], BF16))
        woT = es.enter_context(nc.sbuf_tensor("woT", [128, JC, D], BF16))

        for ic in range(8, 16):
            wio_prep(ic, woT)

        for m in range(NM + 1):
            if m < NM:
                if m == 0:
                    nc.tensor.wait_ge(s_vt.h, NT)
                    nc.tensor.wait_ge(s_pt.h, PT_TOTAL)
                if m >= 2:
                    nc.tensor.wait_ge(s_att.h, m - 1)
                ab = bank(2 * (m % 2), 1024)
                for h in range(NH):
                    for di in range(NDELTA):
                        inst = nc.tensor.matmul(
                            ab[:, 128 * h:128 * (h + 1)],
                            ptab[:, h * NDELTA + di, :],
                            vtok[:, m + di, 128 * h:128 * (h + 1)],
                            start=(di == 0), stop=(di == NDELTA - 1))
                s_aps.inc(inst)
                nc.vector.wait_ge(s_aps.h, m + 1)
                s_att.inc(nc.vector.tensor_copy(att_sb[:, m % 2, :], ab))
            if m >= 1:
                mm = m - 1
                nc.tensor.wait_ge(s_att.h, mm + 1)
                if mm >= 2:
                    nc.tensor.wait_ge(s_attT.h, mm - 1)
                tb = bank_bf(4 + mm % 2)
                for jc in range(JC):
                    inst = nc.tensor.transpose(tb[:, 128 * jc:128 * (jc + 1)],
                                               att_sb[:, mm % 2, 128 * jc:128 * (jc + 1)],
                                               ident[:, :])
                s_atp.inc(inst)
                nc.vector.wait_ge(s_atp.h, mm + 1)
                dstT = attT[:, :, 128 * mm:128 * (mm + 1)]
                s_attT.inc(nc.vector.tensor_copy(dstT, tb.rearrange("p (a f) -> p a f", a=JC)))

        for m in range(NM):
            if m == 0:
                nc.tensor.wait_ge(s_att.h, NM)
                nc.tensor.wait_ge(s_wiot.h, 16)
            nc.tensor.wait_ge(s_attT.h, m + 1)
            if m >= 2:
                nc.tensor.wait_ge(s_fv.h, 2 * (m - 2) + 1)
            for ih in range(2):
                sb = bank(2 * (m % 2) + ih)
                for jc in range(JC):
                    inst = nc.tensor.matmul(
                        sb, attT[:, jc, 128 * m:128 * (m + 1)],
                        woT[:, jc, 512 * ih:512 * (ih + 1)],
                        start=(jc == 0), stop=(jc == JC - 1))
            s_sps.inc(inst)
            if m == 0:
                nc.scalar.wait_ge(s_p0v.h, 5 * NT)
            nc.scalar.wait_ge(s_gacc.h, 8 * E * (m // 4 + 1))
            if m >= 2:
                nc.scalar.wait_ge(s_fv.h, 2 * (m - 2) + 2)
            s_fs.inc(nc.scalar.activation(xc[:, m % 2, :], g_sb[:, m, :], AF.Silu))
            nc.vector.wait_ge(s_sps.h, m + 1)
            nc.vector.wait_ge(s_fs.h, m + 1)
            if m >= 2:
                nc.vector.wait_ge(s_out[m % 2].h, 16 * (m // 2))
            s_fv.inc(nc.vector.tensor_tensor(xc[:, m % 2, :], bank(2 * (m % 2), 1024),
                                             xc[:, m % 2, :], op=ALU.mult))
            nc.vector.wait_ge(s_fv.h, 2 * m + 1)
            nc.vector.wait_ge(s_x[m % 2].h, x_base[m % 2] + 16 * (m // 2 + 1))
            s_fv.inc(nc.vector.tensor_tensor(x_t[:, m % 2, :], xc[:, m % 2, :],
                                             x_t[:, m % 2, :], op=ALU.add))

        if DEBUG:
            nc.sync.wait_ge(s_gacc.h, NPASS * E * 8)
            s_out[0].inc(nc.sync.dma_start(dbg_g[:, :, :], g_sb[:, :, :]), 16)
            nc.sync.wait_ge(s_vt.h, NT)
            s_out[0].inc(nc.sync.dma_start(dbg_vtok[:, :, :], vtok[:, :, :]), 16)
            nc.sync.wait_ge(s_attT.h, NM)
            s_out[0].inc(nc.sync.dma_start(dbg_attT[:, :, :], attT[:, :, :]), 16)
        nc.sync.wait_ge(s_out[0].h, s_out[0].n)
        nc.sync.wait_ge(s_out[1].h, s_out[1].n)

    nc.finalize()
    return nc


def kernel(L0_attn_values, L0_attn_input_weights, L0_attn_output_weight,
           L0_ffn_Wg, L0_ffn_Wu, L0_ffn_Wd):
    x = np.ascontiguousarray(np.asarray(L0_attn_values, dtype=np.float32))
    wi = np.ascontiguousarray(np.asarray(L0_attn_input_weights, dtype=np.float32))
    wo = np.ascontiguousarray(np.asarray(L0_attn_output_weight, dtype=np.float32))
    wg = np.ascontiguousarray(np.asarray(L0_ffn_Wg, dtype=np.float32))
    wu = np.ascontiguousarray(np.asarray(L0_ffn_Wu, dtype=np.float32))
    wd = np.ascontiguousarray(np.asarray(L0_ffn_Wd, dtype=np.float32))
    B, L, _ = x.shape

    nc = build_program()
    in_maps = []
    for c in range(8):
        b, s0 = c // 4, (c % 4) * R
        xs = np.zeros((RT, D), dtype=np.float32)
        lo = max(0, s0 - HALO_L)
        xs[HALO_L - (s0 - lo):HALO_L] = x[b, lo:s0]
        hi = min(L, s0 + R + HALO_R)
        xs[HALO_L:HALO_L + (hi - s0)] = x[b, s0:hi]
        in_maps.append({"x": xs, "wi": wi, "wo": wo, "wg": wg, "wu": wu, "wd": wd})

    res = run_bass_kernel_spmd(nc, in_maps, core_ids=list(range(8)))
    out = np.empty((B, L, D), dtype=np.float32)
    for c in range(8):
        b, s0 = c // 4, (c % 4) * R
        out[b, s0:s0 + R] = res.results[c]["out"]
    return out
